# version 03
# baseline (speedup 1.0000x reference)
import contextlib
import numpy as np
import ml_dtypes
import concourse.bass as bass
import concourse.mybir as mybir
from concourse.bass_utils import run_bass_kernel_spmd

F32 = mybir.dt.float32
BF16 = mybir.dt.bfloat16
I8 = mybir.dt.int8
AF = mybir.ActivationFunctionType
ALU = mybir.AluOpType
AX = mybir.AxisListType.X

TOK = 2048
CD = 2048
FC = 16
TC = 16
H = 16
D = 128
NI = 8192
EPS_LN = 1e-5
EPS_E = float(np.finfo(np.float32).eps)
EPS_R = 1e-6
QEPS = 1e-5
ATT_SCALE = float(128.0 ** -0.5)
NG = 4
GT = 4
SBUF_BASE = 16512
SBUF_LIMIT = 229376

W_QKV, W_PROJ, W_G, W_U, W_D = 0, 1, 2, 3, 4
W_ELEMS = [CD * 3 * CD, CD * CD, CD * NI, CD * NI, NI * CD]


class Sem:
    def __init__(self, es, nc, name):
        self.h = es.enter_context(nc.semaphore(name))
        self.n = 0
        self.name = name

    def inc(self, ins, k=1):
        ins.then_inc(self.h, k)
        self.n += k
        return self.n


class Arena:
    def __init__(self, base, limit):
        self.off = base
        self.limit = limit

    def take(self, nbytes, align=64):
        self.off = -(-self.off // align) * align
        o = self.off
        self.off += nbytes
        assert self.off <= self.limit, f"SBUF overflow: {self.off} > {self.limit}"
        return o


def build_nc(upto=5, debug=False):
    nc = bass.Bass("TRN2", target_bir_lowering=False, debug=False)
    es = contextlib.ExitStack()

    def inp(name, shape, dt=F32):
        return nc.dram_tensor(name, shape, dt, kind="ExternalInput").ap()

    x_own = inp("x_own", [TOK, CD])
    x_pre = inp("x_pre", [TOK, CD])
    wqkvT = inp("wqkvT", [CD, 3 * CD])
    wprojT = inp("wprojT", [CD, CD])
    wgT = inp("wgT", [CD, NI])
    wuT = inp("wuT", [CD, NI])
    wdT = inp("wdT", [NI, CD])
    qnw_in = inp("qnw_b", [128, CD])
    lnw_in = inp("lnw_b", [128, NI])
    nw_in = inp("nw_b", [128, CD])
    nwcm_in = inp("nw_cm", [128, FC])
    pnwcm_in = inp("pnw_cm", [128, FC])
    sb_in = inp("s_b", [128, H])
    ident_in = inp("identf", [128, 128])
    diff_in = inp("diffc", [128, 128])
    trit_in = inp("tritc", [128, 128])
    col_in = inp("colc", [128, 128])
    rev_in = inp("revc", [128, 128])

    out = nc.dram_tensor("out", [TOK, CD], F32, kind="ExternalOutput").ap()
    dbg = {}
    if debug:
        def dout(name, shape, dt=F32):
            dbg[name] = nc.dram_tensor("dbg_" + name, shape, dt, kind="ExternalOutput").ap()
        dout("hpre", [128, 32768])
        dout("hattn", [128, 32768])
        dout("qx", [TC, CD, 128], I8)
        dout("invs", [128, TC])
        dout("S", [128, H * D])
        dout("scal", [1, 64])
        dout("macc", [128, 8])
        dout("ternp", [128, FC * 512])
        dout("aout", [128, 32768])
        dout("scr", [TOK, CD])

    qx_cache = nc.dram_tensor("qx_cache", [TC, CD, 128], I8, kind="Internal").ap()
    scratch = nc.dram_tensor("scratch", [TOK, CD], F32, kind="Internal").ap()
    ternCache = {
        W_G: nc.dram_tensor("ternG", [CD, NI], I8, kind="Internal").ap(),
        W_U: nc.dram_tensor("ternU", [CD, NI], I8, kind="Internal").ap(),
        W_D: nc.dram_tensor("ternD", [NI, CD], I8, kind="Internal").ap(),
    }
    wsrc = {W_G: wgT, W_U: wuT, W_D: wdT}
    bscr = nc.dram_tensor("bscr", [1, 64], F32, kind="Internal").ap()

    def sbt(name, shape, dt, off):
        return nc.alloc_sbuf_tensor_at(name, shape, dt, offset=off).ap()

    pers = Arena(SBUF_BASE, SBUF_LIMIT)
    A = sbt("A", [128, 32768], BF16, pers.take(65536))
    B = sbt("Bb", [128, 32768], BF16, pers.take(65536))
    S_all = sbt("S_all", [128, H * D], F32, pers.take(8192))
    mst_off = pers.take(16384)
    mstage = sbt("mstage", [128, 2 * 2048], F32, mst_off)
    lnwbf = sbt("lnwbf", [128, NI], BF16, mst_off)
    identb = sbt("identb", [128, 128], BF16, pers.take(256))
    diffc = sbt("diffc", [128, 128], F32, pers.take(512))
    identf = diffc
    tritc = sbt("tritc", [128, 128], F32, pers.take(512))
    colc = sbt("colc", [128, 128], F32, pers.take(512))
    revc = sbt("revc", [128, 128], F32, pers.take(512))
    s_b = sbt("s_b", [128, H], F32, pers.take(64))

    def cols(name, n):
        base = pers.take(32 * n, align=32)
        return [sbt(f"{name}{i}", [128, 1], F32, base + 32 * i) for i in range(n)]

    negs = cols("negs", H)
    inva = cols("inva", TC)
    invg = cols("invg", TC)
    invu = cols("invu", TC)
    invd = cols("invd", TC)
    scb = cols("scb", 10)
    macc = sbt("macc", [128, 8], F32, pers.take(32))
    nwcm = sbt("nwcm", [128, FC], F32, pers.take(64))
    pnwcm = sbt("pnwcm", [128, FC], F32, pers.take(64))
    cvc = cols("cvc", FC)
    rt = cols("rt", 32)
    svec = sbt("svec", [1, 64], F32, pers.take(256))
    dbg_invs = sbt("dbg_invs_t", [128, TC], F32, pers.take(64))
    OVB = pers.off

    a1 = Arena(OVB, SBUF_LIMIT)
    xst = sbt("xst", [128, 2 * CD], F32, a1.take(16384))
    xn = sbt("xn", [128, CD], F32, a1.take(8192))
    qnw_b = sbt("qnw_bt", [128, CD], F32, a1.take(8192))
    htmp = sbt("htmp", [128, CD], BF16, a1.take(4096))
    qxi8 = sbt("qxi8", [128, CD], I8, a1.take(2048))
    qxstg = sbt("qxstg", [128, CD], I8, a1.take(2048))

    a2 = Arena(OVB, SBUF_LIMIT)
    wst2 = sbt("wst2", [128, 4096], F32, a2.take(16384))
    wtern2 = sbt("wtern2", [128, 4096], BF16, a2.take(8192))
    wi82 = sbt("wi82", [128, 2048], I8, a2.take(2048))
    kvfm = sbt("kvfm", [128, 4096], BF16, a2.take(8192))
    mask2 = sbt("mask2", [128, 3 * 128], F32, a2.take(1536))
    ktmp2 = sbt("ktmp2", [128, 6 * 128], BF16, a2.take(1536))

    a3 = Arena(OVB, SBUF_LIMIT)
    wst3 = sbt("wst3", [128, 3072], F32, a3.take(12288))
    wtern3 = sbt("wtern3", [128, 6144], BF16, a3.take(12288))
    wi83 = sbt("wi83", [128, 1024], I8, a3.take(1024))
    qkvb = sbt("qkvb", [128, 6144], BF16, a3.take(12288))
    mask3 = sbt("mask3", [128, 3 * 128], F32, a3.take(1536))
    ktmp3 = sbt("ktmp3", [128, 6 * 128], BF16, a3.take(1536))

    a4 = Arena(OVB, SBUF_LIMIT)
    sq4 = sbt("sq4", [128, CD], F32, a4.take(8192))
    t24 = sbt("t24", [128, CD], F32, a4.take(8192))
    sct4 = sbt("sct4", [128, CD], BF16, a4.take(4096))
    nw2b = sbt("nw2b", [128, CD], BF16, a4.take(4096))
    ternP = sbt("ternP", [128, FC * 512], BF16, a4.take(16384))
    wst4 = sbt("wst4", [128, 2 * 512], F32, a4.take(4096))
    wi84 = sbt("wi84", [128, 512], I8, a4.take(512))
    stg4 = sbt("stg4", [128, 2 * 512], F32, a4.take(4096))

    a5 = Arena(OVB, SBUF_LIMIT)
    ternW = sbt("ternW", [128, 2 * 2048], BF16, a5.take(8192))
    qxg = sbt("qxg", [128, GT * CD], BF16, a5.take(16384))
    gub_off = a5.take(4096)
    gub = sbt("gub", [128, GT * 512], BF16, gub_off)
    g0f32 = sbt("g0f32", [128, 512], F32, gub_off)
    qhi8 = sbt("qhi8", [128, NI], I8, a5.take(8192))
    stg5 = sbt("stg5", [128, 4 * 512], F32, a5.take(8192))

    psA = es.enter_context(nc.psum_tensor("psA", [128, 512], F32)).ap()
    psB = es.enter_context(nc.psum_tensor("psB", [128, 512], F32)).ap()
    psC = es.enter_context(nc.psum_tensor("psC", [128, 512], F32)).ap()
    psDp = es.enter_context(nc.psum_tensor("psD", [128, 512], F32)).ap()
    psE = es.enter_context(nc.psum_tensor("psE", [128, 512], F32)).ap()
    psF = es.enter_context(nc.psum_tensor("psF", [128, 512], F32)).ap()
    psT = [es.enter_context(nc.psum_tensor("psT1", [128, 1024], BF16)).ap(),
           es.enter_context(nc.psum_tensor("psT2", [128, 1024], BF16)).ap()]

    smQ = Sem(es, nc, "smQ")
    smG = Sem(es, nc, "smG")
    smPE = Sem(es, nc, "smPE")
    smDV = Sem(es, nc, "smDV")
    smSC = Sem(es, nc, "smSC")
    smOut = Sem(es, nc, "smOut")

    def wt(eng, sem, v):
        assert 0 <= v <= sem.n, f"wait {sem.name}>={v} but emitted {sem.n}"
        if v > 0:
            eng.wait_ge(sem.h, v)

    block = es.enter_context(nc.Block())

    @block.gpsimd
    def _(_gp):
        sy, sc, te, gp = nc.sync, nc.scalar, nc.tensor, nc.gpsimd
        _realve = nc.vector
        onescol = tritc[:, 127:128]

        _auto = set()
        _auto_refs = []
        _serial = {"on": False}

        class _VE:
            def __getattr__(self, name):
                fn = getattr(_realve, name)
                if name in ("wait_ge", "engine_nop"):
                    return fn

                def wrap(*a, **k):
                    ins = fn(*a, **k)
                    smDV.n += 1
                    ins.then_inc(smDV.h, 1)
                    _auto.add(id(ins))
                    _auto_refs.append(ins)
                    if _serial["on"]:
                        _realve.wait_ge(smDV.h, smDV.n)
                    return ins
                return wrap

        ve = _VE()

        class vserial:
            def __enter__(self):
                self.prev = _serial["on"]
                _serial["on"] = True

            def __exit__(self, *a):
                _serial["on"] = self.prev

        def vsw():
            _realve.wait_ge(smDV.h, smDV.n)
        _orig_inc = smDV.inc

        def _smdv_inc(ins, k=1):
            if id(ins) in _auto:
                return smDV.n
            return _orig_inc(ins, k)
        smDV.inc = _smdv_inc

        def vred(*a, **k):
            _realve.wait_ge(smDV.h, smDV.n)
            ins = ve.tensor_reduce(*a, **k)
            _realve.wait_ge(smDV.h, smDV.n)
            return ins

        def vrec(*a):
            _realve.wait_ge(smDV.h, smDV.n)
            ins = ve.reciprocal(*a)
            _realve.wait_ge(smDV.h, smDV.n)
            return ins

        def vrec2(dst, src, t1, t2):
            vrec(dst, src)
            ve.tensor_tensor(t1, src, dst, ALU.mult)
            vsw()
            ve.tensor_scalar(t2, t1, -1.0, 2.0, ALU.mult, ALU.add)
            vsw()
            ve.tensor_tensor(dst, dst, t2, ALU.mult)
            vsw()

        def qdma(out_, in_):
            return smQ.inc(sy.dma_start(out=out_, in_=in_), 16)

        def gdma(out_, in_):
            return smG.inc(gp.dma_start(out=out_, in_=in_), 16)

        def phase_barrier():
            for e in (gp, sy, ve, sc, te):
                wt(e, smPE, smPE.n)
                wt(e, smDV, smDV.n)
                wt(e, smSC, smSC.n)

        ic0 = gdma(identf, ident_in)
        wt(ve, smG, ic0)
        icv = smDV.inc(ve.tensor_copy(identb, identf))
        wt(gp, smDV, icv)
        gdma(diffc, diff_in)
        gdma(tritc, trit_in)
        gdma(colc, col_in)
        gdma(revc, rev_in)
        gdma(s_b, sb_in)
        gdma(qnw_b, qnw_in)
        gdma(nwcm, nwcm_in)
        gdma(pnwcm, pnwcm_in)
        c_init = smG.n
        ve.memset(S_all, 0.0)
        ve.memset(macc, 0.0)
        wt(ve, smG, c_init)
        _serial["on"] = True
        ve.memset(rt[28], EPS_LN)
        ve.memset(rt[29], EPS_E)
        ve.memset(rt[30], EPS_R)
        for h in range(H):
            v = ve.tensor_scalar(negs[h], s_b[:, h:h + 1], -1.0, None, ALU.mult)
        smDV.inc(v)
        _serial["on"] = False

        mviews = [wqkvT.rearrange("(a p) c -> a p c", p=128),
                  wprojT.rearrange("(a p) c -> a p c", p=128),
                  wgT.rearrange("(a p) c -> a p c", p=128),
                  wuT.rearrange("(a p) c -> a p c", p=128),
                  wdT.rearrange("(a p) c -> a p c", p=128)]
        mtiles = []
        for m, mv in enumerate(mviews):
            for a in range(mv.shape[0]):
                for j in range(mv.shape[2] // 2048):
                    mtiles.append((m, a, j))
        mst = {"dma_i": 0, "red_i": 0, "pend": [], "qcnt": [], "bcast": [0] * 5}

        def mean_step(k):
            for _ in range(k):
                if mst["dma_i"] < len(mtiles):
                    i = mst["dma_i"]
                    m, a, j = mtiles[i]
                    slot = i % 2
                    if i >= 2:
                        wt(sy, smDV, mst["pend"][i - 2])
                    c = qdma(mstage[:, slot * 2048:(slot + 1) * 2048],
                             mviews[m][a, :, j * 2048:(j + 1) * 2048])
                    mst["qcnt"].append(c)
                    mst["dma_i"] += 1
                if mst["red_i"] < mst["dma_i"]:
                    i = mst["red_i"]
                    slot = i % 2
                    wt(ve, smQ, mst["qcnt"][i])
                    vred(rt[20],
                                     mstage[:, slot * 2048:(slot + 1) * 2048],
                                     AX, ALU.add, apply_absolute_value=True)
                    m = mtiles[i][0]
                    v = ve.tensor_tensor(macc[:, m:m + 1], macc[:, m:m + 1],
                                         rt[20], ALU.add)
                    mst["pend"].append(smDV.inc(v))
                    mst["red_i"] += 1

        def mean_finalize(m):
            idx = max(i for i, t in enumerate(mtiles) if t[0] == m)
            assert mst["red_i"] > idx, f"matrix {m} mean not reduced yet"
            wt(te, smDV, smDV.n)
            mm = te.matmul(psF[0:1, 0:1], macc[:, m:m + 1], onescol, start=True, stop=True)
            pc = smPE.inc(mm)
            wt(ve, smPE, pc)
            _serial["on"] = True
            ve.tensor_scalar(rt[21][0:1, :], psF[0:1, 0:1], 1.0 / W_ELEMS[m], None, ALU.mult)
            ve.tensor_scalar(rt[22][0:1, :], rt[21][0:1, :], QEPS, None, ALU.max)
            vrec(rt[23][0:1, :], rt[22][0:1, :])
            ve.tensor_copy(svec[0:1, 2 * m:2 * m + 1], rt[23][0:1, :])
            v = ve.tensor_copy(svec[0:1, 2 * m + 1:2 * m + 2], rt[22][0:1, :])
            _serial["on"] = False
            dc = smDV.inc(v)
            wt(te, smDV, dc)
            wt(gp, smDV, dc)
            c1 = gdma(bscr[0:1, 2 * m:2 * m + 2], svec[0:1, 2 * m:2 * m + 2])
            wt(gp, smG, c1)
            gdma(scb[2 * m], bscr[0:1, 2 * m:2 * m + 1].partition_broadcast(128))
            mst["bcast"][m] = gdma(scb[2 * m + 1],
                                   bscr[0:1, 2 * m + 1:2 * m + 2].partition_broadcast(128))

        def scale_ap(m):
            return scb[2 * m]

        trs = {"bank": 0, "evc": [0, 0]}

        def pe_transpose(src_bf16):
            b = trs["bank"]
            wt(te, smDV, trs["evc"][b])
            mm = te.transpose(psT[b][:, 0:128], src_bf16, identb)
            pc = smPE.inc(mm)
            trs["bank"] = 1 - b
            return psT[b][:, 0:128], pc, b

        def tr_evac(dst, ap, pc, b, scale=None):
            wt(ve, smPE, pc)
            v = ve.tensor_copy(dst, ap)
            if scale is not None:
                v = ve.tensor_scalar(dst, dst, scale, None, ALU.mult)
            trs["evc"][b] = smDV.inc(v)
            return trs["evc"][b]

        def ternarize(dst_bf, src_f32, m, i8buf, ncols):
            wt(ve, smG, mst["bcast"][m])
            step = i8buf.shape[1]
            last = 0
            for o in range(0, ncols, step):
                w = min(step, ncols - o)
                ve.tensor_scalar(src_f32[:, o:o + w], src_f32[:, o:o + w],
                                 scale_ap(m), 1.25, ALU.mult, ALU.min)
                vsw()
                ve.tensor_scalar(i8buf[:, 0:w], src_f32[:, o:o + w],
                                 -1.25, None, ALU.max)
                vsw()
                v = ve.tensor_copy(dst_bf[:, o:o + w], i8buf[:, 0:w])
                last = smDV.inc(v)
            return last

        ph1 = {"cons": {}, "hpe": 0}

        def emit_ln_chunk(tc_, prefix, dstH):
            slot = tc_ % 2
            xsrc = x_pre if prefix else x_own
            prev = ph1["cons"].get((prefix, tc_ - 2))
            if prev:
                wt(gp, smDV, prev[0])
                wt(gp, smSC, prev[1])
            xc = gdma(xst[:, slot * CD:(slot + 1) * CD],
                      xsrc[tc_ * 128:(tc_ + 1) * 128, :])
            xs = xst[:, slot * CD:(slot + 1) * CD]
            _serial["on"] = True
            wt(ve, smG, xc)
            vred(rt[0], xs, AX, ALU.add)
            wt(sc, smG, xc)
            wt(sc, smPE, ph1["hpe"])
            s1 = sc.activation(htmp, xs, AF.Square, accum_out=rt[1])
            scc = smSC.inc(s1)
            ve.tensor_scalar(rt[2], rt[0], 1.0 / CD, None, ALU.mult)
            ve.tensor_tensor(rt[3], rt[2], rt[2], ALU.mult)
            wt(ve, smSC, scc)
            ve.tensor_scalar(rt[4], rt[1], 1.0 / CD, None, ALU.mult)
            v = ve.tensor_tensor(rt[4], rt[4], rt[3], ALU.subtract)
            dv1 = smDV.inc(v)
            wt(sc, smDV, dv1)
            sc2 = smSC.inc(sc.activation(rt[5], rt[4], AF.Sqrt, bias=rt[28]))
            wt(ve, smSC, sc2)
            vrec(rt[6], rt[5])
            ve.tensor_scalar(xn, xs, rt[2], None, ALU.subtract)
            ve.tensor_scalar(xn, xn, rt[6], None, ALU.mult)
            ve.tensor_tensor(rt[7], rt[4], rt[6], ALU.mult)
            v = ve.tensor_tensor(rt[7], rt[7], rt[6], ALU.mult)
            dv2 = smDV.inc(v)
            wt(sc, smDV, dv2)
            sc3 = smSC.inc(sc.activation(rt[8], rt[7], AF.Sqrt, bias=rt[29]))
            wt(ve, smSC, sc3)
            vrec(rt[9], rt[8])
            v = ve.scalar_tensor_tensor(htmp, xn, rt[9], qnw_b, ALU.mult, ALU.mult)
            dvh = smDV.inc(v)
            wt(te, smDV, dvh)
            for fcc in range(FC):
                ap, pc, b = pe_transpose(htmp[:, fcc * 128:(fcc + 1) * 128])
                tr_evac(dstH[:, fcc * 2048 + tc_ * 128: fcc * 2048 + (tc_ + 1) * 128],
                        ap, pc, b)
            ph1["hpe"] = smPE.n
            if not prefix:
                vred(rt[10], xn, AX, ALU.max, apply_absolute_value=True)
                ve.tensor_scalar(rt[11], rt[10], QEPS, None, ALU.max)
                vrec2(rt[12], rt[11], rt[17], rt[18])
                ve.tensor_scalar(rt[13], rt[12], 127.0, None, ALU.mult)
                ve.tensor_scalar(inva[tc_], rt[11], 1.0 / 127.0, None, ALU.mult)
                ve.tensor_scalar(qxi8, xn, rt[13], None, ALU.mult)
                wt(ve, smPE, ph1["hpe"])
                dvq = smDV.inc(ve.tensor_copy(htmp, qxi8))
                wt(te, smDV, dvq)
                for fcc in range(FC):
                    ap, pc, b = pe_transpose(htmp[:, fcc * 128:(fcc + 1) * 128])
                    tr_evac(qxstg[:, fcc * 128:(fcc + 1) * 128], ap, pc, b)
                ph1["hpe"] = smPE.n
                wt(gp, smDV, smDV.n)
                gdma(qx_cache[tc_].rearrange("(c p) t -> p c t", p=128),
                     qxstg.rearrange("p (c t) -> p c t", c=FC))
            _serial["on"] = False
            ph1["cons"][(prefix, tc_)] = (smDV.n, smSC.n)

        at = {"wq": 0, "qkv_sc": 0, "qkv_dv": 0}

        def emit_attn_head(h, prefix):
            wst = wst2 if prefix else wst3
            wtern = wtern2 if prefix else wtern3
            wi8 = wi82 if prefix else wi83
            msk = mask2 if prefix else mask3
            ktp = ktmp2 if prefix else ktmp3
            hsrc = A if prefix else B
            ncolw = 256 if prefix else 384
            base = h * 384 + (128 if prefix else 0)
            wt(ve, smPE, smPE.n)
            wt(sc, smDV, smDV.n)
            nh = 1 if prefix else 2
            ck = FC // nh
            for hf in range(nh):
                wt(sy, smDV, at["wq"])
                wc = qdma(wst[:, 0:ck * ncolw].rearrange("p (c o) -> p c o", c=ck),
                          wqkvT[hf * ck * 128:(hf + 1) * ck * 128,
                                base:base + ncolw].rearrange("(c p) o -> p c o", p=128))
                wt(ve, smQ, wc)
                at["wq"] = ternarize(wtern[:, hf * ck * ncolw:(hf + 1) * ck * ncolw],
                                     wst[:, 0:ck * ncolw], W_QKV, wi8, ck * ncolw)
            nsh = negs[h]
            if not prefix:
                sc.activation(msk[:, 0:128], diffc, AF.Exp, scale=nsh)
                sc.activation(msk[:, 128:256], colc, AF.Exp, scale=nsh)
            sc.activation(msk[:, 256:384], revc, AF.Exp, scale=nsh)
            scm = smSC.inc(sc.activation(rt[16], s_b[:, h:h + 1], AF.Exp,
                                         scale=-128.0))
            wt(ve, smSC, scm)
            if not prefix:
                smDV.inc(ve.tensor_tensor(msk[:, 0:128], msk[:, 0:128], tritc, ALU.mult))
                vsw()
            js = (1, 2) if prefix else (0, 1, 2)
            dstb = kvfm if prefix else qkvb
            pss = [psA, psB, psC]
            wt(te, smDV, at["wq"])
            wt(te, smSC, scm)
            for tb in range(4):
                for ji, j in enumerate(js):
                    off = j - (1 if prefix else 0)
                    for kc in range(FC):
                        mm = te.matmul(
                            pss[ji][:, 0:512],
                            wtern[:, kc * ncolw + off * 128: kc * ncolw + off * 128 + 128],
                            hsrc[:, kc * 2048 + tb * 512: kc * 2048 + tb * 512 + 512],
                            start=(kc == 0), stop=(kc == FC - 1))
                    pc = smPE.inc(mm)
                    dst = dstb[:, off * 2048 + tb * 512: off * 2048 + tb * 512 + 512]
                    if ji % 2 == 0:
                        wt(sc, smPE, pc)
                        wt(sc, smG, mst["bcast"][W_QKV])
                        at["qkv_sc"] = smSC.inc(sc.activation(dst, pss[ji][:, 0:512], AF.Copy,
                                                              scale=scb[2 * W_QKV + 1]))
                    else:
                        wt(ve, smPE, pc)
                        at["qkv_dv"] = smDV.inc(ve.tensor_scalar(dst, pss[ji][:, 0:512],
                                                                 scb[2 * W_QKV + 1], None, ALU.mult))
                wt(te, smSC, at["qkv_sc"])
                wt(te, smDV, at["qkv_dv"])
            koff = 0 if prefix else 2048
            voff = 2048 if prefix else 4096
            qt_sl = ktp[:, 4 * 128:5 * 128]
            sb_sl = ktp[:, 5 * 128:6 * 128]
            for c in range(TC):
                cb = c * 128
                ap, pc, b = pe_transpose(dstb[:, voff + cb: voff + cb + 128])
                tr_evac(ktp[:, 0:128], ap, pc, b)
                v = ve.tensor_tensor(ktp[:, 128:256],
                                     dstb[:, koff + cb: koff + cb + 128],
                                     msk[:, 256:384], ALU.mult)
                kdv = smDV.inc(v)
                wt(te, smDV, kdv)
                ap2, pc2, b2 = pe_transpose(ktp[:, 128:256])
                kevc = tr_evac(ktp[:, 256:384], ap2, pc2, b2)
                if not prefix:
                    mm = te.matmul(psDp[:, 0:128], dstb[:, 2048 + cb: 2048 + cb + 128],
                                   dstb[:, cb: cb + 128], start=True, stop=True)
                    pcs = smPE.inc(mm)
                    wt(ve, smPE, pcs)
                    ve.tensor_tensor(ktp[:, 384:512], psDp[:, 0:128],
                                     msk[:, 0:128], ALU.mult)
                    ve.tensor_tensor(qt_sl, dstb[:, cb: cb + 128],
                                     msk[:, 128:256], ALU.mult)
                    dsb = smDV.inc(ve.tensor_copy(sb_sl, S_all[:, h * 128:(h + 1) * 128]))
                    wt(te, smDV, dsb)
                    te.matmul(psE[:, 0:128], ktp[:, 384:512], ktp[:, 0:128],
                              start=True, stop=False)
                    mm = te.matmul(psE[:, 0:128], qt_sl, sb_sl, start=False, stop=True)
                    pco = smPE.inc(mm)
                    wt(sc, smPE, pco)
                    smSC.inc(sc.activation(A[:, c * 2048 + h * 128: c * 2048 + (h + 1) * 128],
                                           psE[:, 0:128], AF.Copy, scale=ATT_SCALE))
                wt(te, smDV, kevc)
                mm = te.matmul(psF[:, 0:128], ktp[:, 256:384], ktp[:, 0:128],
                               start=True, stop=True)
                pkv = smPE.inc(mm)
                wt(ve, smPE, pkv)
                v = ve.scalar_tensor_tensor(S_all[:, h * 128:(h + 1) * 128],
                                            S_all[:, h * 128:(h + 1) * 128],
                                            rt[16], psF[:, 0:128],
                                            ALU.mult, ALU.add)
                dkv = smDV.inc(v)
                vsw()
                wt(te, smDV, dkv)
                wt(te, smSC, smSC.n)

        for tci in range(TC):
            emit_ln_chunk(tci, True, A)
            mean_step(3)
        mean_finalize(W_QKV)
        phase_barrier()

        for h in range(H):
            emit_attn_head(h, True)
            mean_step(6)
        mean_finalize(W_PROJ)
        phase_barrier()

        gdma(qnw_b, qnw_in)
        for tci in range(TC):
            emit_ln_chunk(tci, False, B)
            mean_step(5)
        mean_finalize(W_G)
        qx_done = smG.n
        phase_barrier()

        if upto >= 2:
            for h in range(H):
                emit_attn_head(h, False)
                mean_step(9)
            mean_step(len(mtiles))
            mean_finalize(W_U)
            mean_finalize(W_D)
            phase_barrier()

        scratch_done = 0
        if upto >= 3:
            wt(gp, smG, mst["bcast"][W_PROJ])
            xc = gdma(sq4, nw_in)
            wt(sc, smG, xc)
            scn = smSC.inc(sc.activation(nw2b, sq4, AF.Square))
            _serial["on"] = True
            for fcc in range(FC):
                v = ve.tensor_tensor(cvc[fcc], nwcm[:, fcc:fcc + 1],
                                     pnwcm[:, fcc:fcc + 1], ALU.mult)
            smDV.inc(v)
            _serial["on"] = False
            ph4sc = {"sq": 0}
            for tc_ in range(TC):
                asl = A[:, tc_ * 2048:(tc_ + 1) * 2048]
                wt(sc, smDV, ph4sc["sq"])
                if tc_ == 0:
                    wt(sc, smSC, scn)
                s1 = sc.activation(sq4, asl, AF.Square, accum_out=rt[1])
                scc = smSC.inc(s1)
                wt(ve, smSC, max(scc, scn))
                _serial["on"] = True
                ve.tensor_tensor(t24, sq4, nw2b, ALU.mult)
                vred(rt[2], t24, AX, ALU.add)
                ve.tensor_scalar(rt[3], rt[1], 1.0 / CD, None, ALU.mult)
                v = ve.tensor_scalar(rt[4], rt[2], 1.0 / CD, None, ALU.mult)
                dv1 = smDV.inc(v)
                ph4sc["sq"] = dv1
                wt(sc, smDV, dv1)
                sc2 = smSC.inc(sc.activation(rt[5], rt[3], AF.Sqrt, bias=rt[30]))
                wt(ve, smSC, sc2)
                vrec(rt[6], rt[5])
                ve.tensor_tensor(rt[7], rt[6], rt[6], ALU.mult)
                v = ve.tensor_tensor(rt[8], rt[4], rt[7], ALU.mult)
                dv2 = smDV.inc(v)
                wt(sc, smDV, dv2)
                sc3 = smSC.inc(sc.activation(rt[9], rt[8], AF.Sqrt, bias=rt[29]))
                wt(ve, smSC, sc3)
                vrec(rt[10], rt[9])
                v = ve.tensor_tensor(rt[11], rt[6], rt[10], ALU.mult)
                dv3 = smDV.inc(v)
                _serial["on"] = False
                wt(sc, smDV, dv3)
                s4 = sc.activation(sct4, asl, AF.Copy, scale=rt[11])
                sc4 = smSC.inc(s4)
                wt(te, smSC, sc4)
                for fcc in range(FC):
                    ap, pc, b = pe_transpose(sct4[:, fcc * 128:(fcc + 1) * 128])
                    tr_evac(B[:, fcc * 2048 + tc_ * 128: fcc * 2048 + (tc_ + 1) * 128],
                            ap, pc, b, scale=cvc[fcc])
            ph4w = {"tern": 0}
            slotdma = {0: 0, 1: 0, 2: 0, 3: 0}
            for ob in range(4):
                wt(ve, smPE, smPE.n)
                for kc in range(FC):
                    slot = kc % 2
                    wt(sy, smDV, ph4w["tern"])
                    wcc = qdma(wst4[:, slot * 512:(slot + 1) * 512],
                               wprojT[kc * 128:(kc + 1) * 128, ob * 512:(ob + 1) * 512])
                    wt(ve, smQ, wcc)
                    wt(ve, smG, mst["bcast"][W_PROJ])
                    ve.tensor_scalar(wst4[:, slot * 512:(slot + 1) * 512],
                                     wst4[:, slot * 512:(slot + 1) * 512],
                                     scale_ap(W_PROJ), 1.25, ALU.mult, ALU.min)
                    vsw()
                    ve.tensor_scalar(wi84, wst4[:, slot * 512:(slot + 1) * 512],
                                     -1.25, None, ALU.max)
                    vsw()
                    v = ve.tensor_copy(ternP[:, kc * 512:(kc + 1) * 512], wi84)
                    ph4w["tern"] = smDV.inc(v)
                wt(te, smDV, ph4w["tern"])
                evp = {"sc": 0, "dv": 0}
                for tc_ in range(TC):
                    pst = psA if tc_ % 2 == 0 else psB
                    for kc in range(FC):
                        mm = te.matmul(pst[:, 0:512],
                                       B[:, kc * 2048 + tc_ * 128: kc * 2048 + (tc_ + 1) * 128],
                                       ternP[:, kc * 512:(kc + 1) * 512],
                                       start=(kc == 0), stop=(kc == FC - 1))
                    pc = smPE.inc(mm)
                    xslot = tc_ % 2
                    xc = gdma(stg4[:, xslot * 512:(xslot + 1) * 512],
                              x_own[tc_ * 128:(tc_ + 1) * 128, ob * 512:(ob + 1) * 512])
                    tslot = tc_ % 4
                    wt(ve, smG, xc)
                    wt(ve, smG, slotdma[tslot])
                    wt(ve, smPE, pc)
                    v = ve.scalar_tensor_tensor(t24[:, tslot * 512:(tslot + 1) * 512],
                                                pst[:, 0:512], scb[2 * W_PROJ + 1],
                                                stg4[:, xslot * 512:(xslot + 1) * 512],
                                                ALU.mult, ALU.add)
                    dvv = smDV.inc(v)
                    wt(te, smDV, dvv)
                    wt(gp, smDV, dvv)
                    slotdma[tslot] = gdma(
                        scratch[tc_ * 128:(tc_ + 1) * 128, ob * 512:(ob + 1) * 512],
                        t24[:, tslot * 512:(tslot + 1) * 512])
            scratch_done = smG.n
            phase_barrier()

        if upto >= 4:
            wt(gp, smG, mst["bcast"][W_G])
            wt(gp, smG, mst["bcast"][W_U])
            wt(gp, smG, mst["bcast"][W_D])
            wt(gp, smG, qx_done)
            wt(gp, smG, scratch_done)
            lc = gdma(lnwbf, lnw_in)
            wt(ve, smG, lc)
            _serial["on"] = True
            for i in range(TC):
                ve.tensor_scalar(invg[i], inva[i], scb[2 * W_G + 1], None, ALU.mult)
                v = ve.tensor_scalar(invu[i], inva[i], scb[2 * W_U + 1], None, ALU.mult)
            smDV.inc(v)
            _serial["on"] = False
            cache_ready = {W_G: 0, W_U: 0, W_D: 0}
            slotst = {"pe": [0, 0], "i8dv": 0}
            scrfree = {0: 0, 1: 0}

            def load_rhs_slot(m, slot, kc0, nkc, ob, group):
                dst = ternW[:, slot * 2048: slot * 2048 + nkc * 512]
                wt(gp, smPE, slotst["pe"][slot])
                wt(sy, smPE, slotst["pe"][slot])
                src = wsrc[m]
                cache = ternCache[m]
                if group == 0:
                    f32b = g0f32 if m == W_D else stg5
                    nslots = 1 if m == W_D else 2
                    if m == W_D:
                        wt(sy, smDV, smDV.n)
                    last = slotst.get("g0dv", 0)
                    for i in range(nkc):
                        kc = kc0 + i
                        s2 = i % nslots
                        wt(sy, smDV, last)
                        wcc = qdma(f32b[:, s2 * 512:(s2 + 1) * 512],
                                   src[kc * 128:(kc + 1) * 128, ob * 512:(ob + 1) * 512])
                        wt(ve, smQ, wcc)
                        wt(ve, smG, mst["bcast"][m])
                        wt(ve, smG, slotst["i8dv"])
                        ve.tensor_scalar(f32b[:, s2 * 512:(s2 + 1) * 512],
                                         f32b[:, s2 * 512:(s2 + 1) * 512],
                                         scale_ap(m), 1.25, ALU.mult, ALU.min)
                        vsw()
                        ve.tensor_scalar(qhi8[:, i * 512:(i + 1) * 512],
                                         f32b[:, s2 * 512:(s2 + 1) * 512],
                                         -1.25, None, ALU.max)
                        vsw()
                        v = ve.tensor_copy(dst[:, i * 512:(i + 1) * 512],
                                           qhi8[:, i * 512:(i + 1) * 512])
                        last = smDV.inc(v)
                        wt(gp, smDV, last)
                        cache_ready[m] = gdma(
                            cache[kc * 128:(kc + 1) * 128, ob * 512:(ob + 1) * 512],
                            qhi8[:, i * 512:(i + 1) * 512])
                    slotst["i8dv"] = cache_ready[m]
                    slotst["g0dv"] = last
                    return ("dv", last)
                else:
                    wt(gp, smG, cache_ready[m])
                    c = gdma(dst.rearrange("p (c o) -> p c o", c=nkc),
                             cache[kc0 * 128:(kc0 + nkc) * 128,
                                   ob * 512:(ob + 1) * 512].rearrange(
                                       "(c p) o -> p c o", p=128))
                    return ("g", c)

            for g in range(NG):
                wt(gp, smPE, smPE.n)
                for t_ in range(GT):
                    tc_ = g * GT + t_
                    gdma(qxg[:, t_ * 2048:(t_ + 1) * 2048].rearrange(
                        "p (c t) -> p c t", c=FC),
                        qx_cache[tc_].rearrange("(c p) t -> p c t", p=128))
                qxc = smG.n
                psGU = [psA, psB, psC, psDp]
                for ob in range(FC):
                    hoff = ob * 512
                    for m in (W_G, W_U):
                        wt(te, smG, qxc)
                        for si in range(4):
                            slot = si % 2
                            kind, val = load_rhs_slot(m, slot, si * 4, 4, ob, g)
                            wt(te, smG if kind == "g" else smDV, val)
                            for t_ in range(GT):
                                for i in range(4):
                                    kc = si * 4 + i
                                    mm = te.matmul(
                                        psGU[t_][:, 0:512],
                                        qxg[:, t_ * 2048 + kc * 128: t_ * 2048 + (kc + 1) * 128],
                                        ternW[:, slot * 2048 + i * 512: slot * 2048 + (i + 1) * 512],
                                        start=(kc == 0), stop=(kc == FC - 1))
                            pc = smPE.inc(mm)
                            slotst["pe"][slot] = pc
                        pc = smPE.n
                        for t_ in range(GT):
                            if m == W_G:
                                wt(sc, smPE, pc)
                                wt(sc, smDV, smDV.n)
                                s1 = sc.activation(gub[:, t_ * 512:(t_ + 1) * 512],
                                                   psGU[t_][:, 0:512], AF.Silu,
                                                   scale=invg[g * GT + t_])
                                scc = smSC.inc(s1)
                                wt(te, smSC, scc)
                            else:
                                wt(ve, smPE, pc)
                                wt(ve, smSC, smSC.n)
                                v = ve.scalar_tensor_tensor(
                                    A[:, t_ * NI + hoff: t_ * NI + hoff + 512],
                                    psGU[t_][:, 0:512],
                                    invu[g * GT + t_],
                                    gub[:, t_ * 512:(t_ + 1) * 512],
                                    ALU.mult, ALU.mult)
                                dvv = smDV.inc(v)
                                wt(te, smDV, dvv)
                for t_ in range(GT):
                    asl = A[:, t_ * NI:(t_ + 1) * NI]
                    bsl = B[:, t_ * NI:(t_ + 1) * NI]
                    wt(sc, smDV, smDV.n)
                    s1 = sc.activation(bsl, asl, AF.Square, accum_out=rt[1])
                    scc = smSC.inc(s1)
                    wt(ve, smSC, scc)
                    _serial["on"] = True
                    v = ve.tensor_scalar(rt[2], rt[1], 1.0 / NI, None, ALU.mult)
                    dv1 = smDV.inc(v)
                    wt(sc, smDV, dv1)
                    sc2 = smSC.inc(sc.activation(rt[3], rt[2], AF.Sqrt,
                                                 bias=rt[30]))
                    wt(ve, smSC, sc2)
                    vrec(rt[4], rt[3])
                    ve.scalar_tensor_tensor(asl, asl, rt[4], lnwbf,
                                            ALU.mult, ALU.mult)
                    vred(rt[5], asl, AX, ALU.max,
                                     apply_absolute_value=True)
                    ve.tensor_scalar(rt[6], rt[5], QEPS, None, ALU.max)
                    vrec2(rt[7], rt[6], rt[17], rt[18])
                    ve.tensor_scalar(rt[8], rt[7], 127.0, None, ALU.mult)
                    ve.tensor_scalar(rt[9], rt[6], 1.0 / 127.0, None, ALU.mult)
                    ve.tensor_scalar(invd[g * GT + t_], rt[9],
                                     scb[2 * W_D + 1], None, ALU.mult)
                    wt(ve, smG, slotst["i8dv"])
                    ve.tensor_scalar(qhi8, asl, rt[8], None, ALU.mult)
                    v = ve.tensor_copy(asl, qhi8)
                    _serial["on"] = False
                    dvq = smDV.inc(v)
                    wt(te, smDV, dvq)
                    for ic in range(64):
                        ap, pc, b = pe_transpose(asl[:, ic * 128:(ic + 1) * 128])
                        tr_evac(bsl[:, ic * 128:(ic + 1) * 128], ap, pc, b)
                qh_dv = smDV.n
                psDn = [psA, psB, psC, psDp]
                for ob in range(4):
                    for si in range(16):
                        slot = si % 2
                        kind, val = load_rhs_slot(W_D, slot, si * 4, 4, ob, g)
                        wt(te, smG if kind == "g" else smDV, val)
                        if si == 0:
                            wt(te, smDV, qh_dv)
                        for t_ in range(GT):
                            for i in range(4):
                                ic = si * 4 + i
                                mm = te.matmul(
                                    psDn[t_][:, 0:512],
                                    B[:, t_ * NI + ic * 128: t_ * NI + (ic + 1) * 128],
                                    ternW[:, slot * 2048 + i * 512: slot * 2048 + (i + 1) * 512],
                                    start=(ic == 0), stop=(ic == 63))
                        pc = smPE.inc(mm)
                        slotst["pe"][slot] = pc
                    wt(ve, smPE, smPE.n)
                    for t_ in range(GT):
                        tc_ = g * GT + t_
                        ssl = t_ % 2
                        wt(gp, smDV, scrfree[ssl])
                        xc = gdma(stg5[:, ssl * 512:(ssl + 1) * 512],
                                  scratch[tc_ * 128:(tc_ + 1) * 128,
                                          ob * 512:(ob + 1) * 512])
                        wt(ve, smG, xc)
                        wt(ve, smOut, smOut.n)
                        v = ve.scalar_tensor_tensor(
                            stg5[:, (2 + ssl) * 512:(3 + ssl) * 512],
                            psDn[t_][:, 0:512],
                            invd[tc_],
                            stg5[:, ssl * 512:(ssl + 1) * 512],
                            ALU.mult, ALU.add)
                        dvv = smDV.inc(v)
                        scrfree[ssl] = dvv
                        wt(te, smDV, dvv)
                        wt(gp, smDV, dvv)
                        smOut.inc(gp.dma_start(
                            out=out[tc_ * 128:(tc_ + 1) * 128, ob * 512:(ob + 1) * 512],
                            in_=stg5[:, (2 + ssl) * 512:(3 + ssl) * 512]), 16)

        if debug:
            wt(gp, smDV, smDV.n)
            wt(gp, smSC, smSC.n)
            wt(gp, smPE, smPE.n)
            if upto < 2:
                gdma(dbg["hpre"], A)
            gdma(dbg["hattn"], B)
            gdma(dbg["qx"], qx_cache)
            _serial["on"] = True
            for i in range(TC):
                v = ve.tensor_copy(dbg_invs[:, i:i + 1], inva[i])
            dvv = smDV.inc(v)
            _serial["on"] = False
            wt(gp, smDV, dvv)
            gdma(dbg["invs"], dbg_invs)
            gdma(dbg["S"], S_all)
            gdma(dbg["scal"], svec)
            gdma(dbg["macc"], macc)
            if 2 <= upto <= 3:
                gdma(dbg["aout"], A)
            if upto == 3:
                gdma(dbg["hattn"], B)
                gdma(dbg["ternp"], ternP)
            if upto >= 3:
                gdma(dbg["scr"], scratch)
            wt(gp, smG, smG.n)
        if upto >= 4:
            wt(gp, smOut, smOut.n)
        else:
            ve.memset(xn, 0.0)
            dz = smDV.inc(ve.tensor_scalar(xn, xn, 1.0, None, ALU.mult))
            wt(gp, smDV, dz)
            for tci in range(TC):
                smOut.inc(gp.dma_start(out=out[tci * 128:(tci + 1) * 128, :], in_=xn), 16)
            wt(gp, smOut, smOut.n)

    return nc, es


def host_layout(inputs):
    x = np.asarray(inputs["L0_attn_x"], dtype=np.float32).reshape(16384, CD)
    qkv_w = np.asarray(inputs["L0_attn_qkv_w"], dtype=np.float32)
    proj_w = np.asarray(inputs["L0_attn_proj_w"], dtype=np.float32)
    w_gate = np.asarray(inputs["L0_ffn_w_gate"], dtype=np.float32)
    w_up = np.asarray(inputs["L0_ffn_w_up"], dtype=np.float32)
    w_down = np.asarray(inputs["L0_ffn_w_down"], dtype=np.float32)
    qnw = np.asarray(inputs["L0_attn_qkv_norm_w"], dtype=np.float32)
    pnw = np.asarray(inputs["L0_attn_proj_norm_w"], dtype=np.float32)
    nw = np.asarray(inputs["L0_attn_norm_w"], dtype=np.float32)
    s = np.asarray(inputs["L0_attn_s"], dtype=np.float32)
    lnw = np.asarray(inputs["L0_ffn_ln_weight"], dtype=np.float32)

    perm = np.empty(3 * CD, dtype=np.int64)
    i = 0
    for h in range(H):
        for j in range(3):
            perm[i:i + 128] = j * CD + h * 128 + np.arange(128)
            i += 128
    wqkvT_hm = np.ascontiguousarray(qkv_w[perm].T)

    shared = {
        "wqkvT": wqkvT_hm,
        "wprojT": np.ascontiguousarray(proj_w.T),
        "wgT": np.ascontiguousarray(w_gate.T),
        "wuT": np.ascontiguousarray(w_up.T),
        "wdT": np.ascontiguousarray(w_down.T),
        "qnw_b": np.ascontiguousarray(np.broadcast_to(qnw, (128, CD))),
        "lnw_b": np.ascontiguousarray(np.broadcast_to(lnw, (128, NI))),
        "nw_b": np.ascontiguousarray(np.broadcast_to(nw, (128, CD))),
        "nw_cm": np.ascontiguousarray(nw.reshape(FC, 128).T),
        "pnw_cm": np.ascontiguousarray(pnw.reshape(FC, 128).T),
        "s_b": np.ascontiguousarray(np.broadcast_to(s, (128, H))),
        "identf": np.eye(128, dtype=np.float32),
        "diffc": np.maximum(np.arange(128)[None, :] - np.arange(128)[:, None], 0).astype(np.float32),
        "tritc": (np.arange(128)[None, :] >= np.arange(128)[:, None]).astype(np.float32),
        "colc": np.ascontiguousarray(np.broadcast_to(np.arange(128, dtype=np.float32)[None, :], (128, 128))),
        "revc": np.ascontiguousarray(np.broadcast_to(128.0 - np.arange(128, dtype=np.float32)[None, :], (128, 128))),
    }
    zeros = np.zeros((TOK, CD), dtype=np.float32)
    in_maps = []
    for c in range(8):
        m = dict(shared)
        m["x_own"] = np.ascontiguousarray(x[c * TOK:(c + 1) * TOK])
        m["x_pre"] = (np.ascontiguousarray(x[(c - 1) * TOK:c * TOK])
                      if c % 2 == 1 else zeros)
        in_maps.append(m)
    return in_maps


def kernel(**inputs):
    in_maps = host_layout(inputs)
    nc, es = build_nc(upto=5, debug=False)
    try:
        res = run_bass_kernel_spmd(nc, in_maps, core_ids=list(range(8)))
    finally:
        es.close()
    full = np.concatenate([res.results[c]["out"] for c in range(8)], axis=0)
    return full.reshape(4, 4096, CD)


# version 04
# speedup vs baseline: 1.0007x; 1.0007x over previous
import contextlib
import numpy as np
import ml_dtypes
import concourse.bass as bass
import concourse.mybir as mybir
from concourse.bass_utils import run_bass_kernel_spmd

F32 = mybir.dt.float32
BF16 = mybir.dt.bfloat16
I8 = mybir.dt.int8
AF = mybir.ActivationFunctionType
ALU = mybir.AluOpType
AX = mybir.AxisListType.X

TOK = 2048
CD = 2048
FC = 16
TC = 16
H = 16
D = 128
NI = 8192
EPS_LN = 1e-5
EPS_E = float(np.finfo(np.float32).eps)
EPS_R = 1e-6
QEPS = 1e-5
ATT_SCALE = float(128.0 ** -0.5)
NG = 4
GT = 4
SBUF_BASE = 16512
SBUF_LIMIT = 229376

W_QKV, W_PROJ, W_G, W_U, W_D = 0, 1, 2, 3, 4
W_ELEMS = [CD * 3 * CD, CD * CD, CD * NI, CD * NI, NI * CD]


class Sem:
    def __init__(self, es, nc, name):
        self.h = es.enter_context(nc.semaphore(name))
        self.n = 0
        self.name = name

    def inc(self, ins, k=1):
        ins.then_inc(self.h, k)
        self.n += k
        return self.n


class Arena:
    def __init__(self, base, limit):
        self.off = base
        self.limit = limit

    def take(self, nbytes, align=64):
        self.off = -(-self.off // align) * align
        o = self.off
        self.off += nbytes
        assert self.off <= self.limit, f"SBUF overflow: {self.off} > {self.limit}"
        return o


def build_nc(upto=5, debug=False):
    nc = bass.Bass("TRN2", target_bir_lowering=False, debug=False)
    es = contextlib.ExitStack()

    def inp(name, shape, dt=F32):
        return nc.dram_tensor(name, shape, dt, kind="ExternalInput").ap()

    x_own = inp("x_own", [TOK, CD])
    x_pre = inp("x_pre", [TOK, CD])
    wqkvT = inp("wqkvT", [CD, 3 * CD])
    wprojT = inp("wprojT", [CD, CD])
    wgT = inp("wgT", [CD, NI])
    wuT = inp("wuT", [CD, NI])
    wdT = inp("wdT", [NI, CD])
    qnw_in = inp("qnw_b", [128, CD])
    lnw_in = inp("lnw_b", [128, NI])
    nw_in = inp("nw_b", [128, CD])
    nwcm_in = inp("nw_cm", [128, FC])
    pnwcm_in = inp("pnw_cm", [128, FC])
    sb_in = inp("s_b", [128, H])
    ident_in = inp("identf", [128, 128])
    diff_in = inp("diffc", [128, 128])
    trit_in = inp("tritc", [128, 128])
    col_in = inp("colc", [128, 128])
    rev_in = inp("revc", [128, 128])

    out = nc.dram_tensor("out", [TOK, CD], F32, kind="ExternalOutput").ap()
    dbg = {}
    if debug:
        def dout(name, shape, dt=F32):
            dbg[name] = nc.dram_tensor("dbg_" + name, shape, dt, kind="ExternalOutput").ap()
        dout("hpre", [128, 32768])
        dout("hattn", [128, 32768])
        dout("qx", [TC, CD, 128], I8)
        dout("invs", [128, TC])
        dout("S", [128, H * D])
        dout("scal", [1, 64])
        dout("macc", [128, 8])
        dout("ternp", [128, FC * 512])
        dout("aout", [128, 32768])
        dout("scr", [TOK, CD])

    qx_cache = nc.dram_tensor("qx_cache", [TC, CD, 128], I8, kind="Internal").ap()
    scratch = nc.dram_tensor("scratch", [TOK, CD], F32, kind="Internal").ap()
    ternCache = {
        W_G: nc.dram_tensor("ternG", [CD, NI], I8, kind="Internal").ap(),
        W_U: nc.dram_tensor("ternU", [CD, NI], I8, kind="Internal").ap(),
        W_D: nc.dram_tensor("ternD", [NI, CD], I8, kind="Internal").ap(),
    }
    wsrc = {W_G: wgT, W_U: wuT, W_D: wdT}
    bscr = nc.dram_tensor("bscr", [1, 64], F32, kind="Internal").ap()

    def sbt(name, shape, dt, off):
        return nc.alloc_sbuf_tensor_at(name, shape, dt, offset=off).ap()

    pers = Arena(SBUF_BASE, SBUF_LIMIT)
    A = sbt("A", [128, 32768], BF16, pers.take(65536))
    B = sbt("Bb", [128, 32768], BF16, pers.take(65536))
    S_all = sbt("S_all", [128, H * D], F32, pers.take(8192))
    mst_off = pers.take(16384)
    mstage = sbt("mstage", [128, 2 * 2048], F32, mst_off)
    lnwbf = sbt("lnwbf", [128, NI], BF16, mst_off)
    identb = sbt("identb", [128, 128], BF16, pers.take(256))
    diffc = sbt("diffc", [128, 128], F32, pers.take(512))
    identf = diffc
    tritc = sbt("tritc", [128, 128], F32, pers.take(512))
    colc = sbt("colc", [128, 128], F32, pers.take(512))
    revc = sbt("revc", [128, 128], F32, pers.take(512))
    s_b = sbt("s_b", [128, H], F32, pers.take(64))

    def cols(name, n):
        base = pers.take(32 * n, align=32)
        return [sbt(f"{name}{i}", [128, 1], F32, base + 32 * i) for i in range(n)]

    negs = cols("negs", H)
    inva = cols("inva", TC)
    invg = cols("invg", TC)
    invu = cols("invu", TC)
    invd = cols("invd", TC)
    scb = cols("scb", 10)
    macc = sbt("macc", [128, 8], F32, pers.take(32))
    nwcm = sbt("nwcm", [128, FC], F32, pers.take(64))
    pnwcm = sbt("pnwcm", [128, FC], F32, pers.take(64))
    cvc = cols("cvc", FC)
    rt = cols("rt", 32)
    svec = sbt("svec", [1, 64], F32, pers.take(256))
    dbg_invs = sbt("dbg_invs_t", [128, TC], F32, pers.take(64))
    OVB = pers.off

    a1 = Arena(OVB, SBUF_LIMIT)
    xst = sbt("xst", [128, 2 * CD], F32, a1.take(16384))
    xn = sbt("xn", [128, CD], F32, a1.take(8192))
    qnw_b = sbt("qnw_bt", [128, CD], F32, a1.take(8192))
    htmp = sbt("htmp", [128, CD], BF16, a1.take(4096))
    qxi8 = sbt("qxi8", [128, CD], I8, a1.take(2048))
    qxstg = sbt("qxstg", [128, CD], I8, a1.take(2048))

    a2 = Arena(OVB, SBUF_LIMIT)
    wst2 = sbt("wst2", [128, 4096], F32, a2.take(16384))
    wtern2 = sbt("wtern2", [128, 4096], BF16, a2.take(8192))
    wi82 = sbt("wi82", [128, 2048], I8, a2.take(2048))
    kvfm = sbt("kvfm", [128, 4096], BF16, a2.take(8192))
    mask2 = sbt("mask2", [128, 3 * 128], F32, a2.take(1536))
    ktmp2 = sbt("ktmp2", [128, 6 * 128], BF16, a2.take(1536))

    a3 = Arena(OVB, SBUF_LIMIT)
    wst3 = sbt("wst3", [128, 3072], F32, a3.take(12288))
    wtern3 = sbt("wtern3", [128, 6144], BF16, a3.take(12288))
    wi83 = sbt("wi83", [128, 1024], I8, a3.take(1024))
    qkvb = sbt("qkvb", [128, 6144], BF16, a3.take(12288))
    mask3 = sbt("mask3", [128, 3 * 128], F32, a3.take(1536))
    ktmp3 = sbt("ktmp3", [128, 6 * 128], BF16, a3.take(1536))

    a4 = Arena(OVB, SBUF_LIMIT)
    sq4 = sbt("sq4", [128, CD], F32, a4.take(8192))
    t24 = sbt("t24", [128, CD], F32, a4.take(8192))
    sct4 = sbt("sct4", [128, CD], BF16, a4.take(4096))
    nw2b = sbt("nw2b", [128, CD], BF16, a4.take(4096))
    ternP = sbt("ternP", [128, FC * 512], BF16, a4.take(16384))
    wst4 = sbt("wst4", [128, 2 * 512], F32, a4.take(4096))
    wi84 = sbt("wi84", [128, 512], I8, a4.take(512))
    stg4 = sbt("stg4", [128, 2 * 512], F32, a4.take(4096))

    a5 = Arena(OVB, SBUF_LIMIT)
    ternW = sbt("ternW", [128, 2 * 2048], BF16, a5.take(8192))
    qxg = sbt("qxg", [128, GT * CD], BF16, a5.take(16384))
    gub_off = a5.take(4096)
    gub = sbt("gub", [128, GT * 512], BF16, gub_off)
    g0f32 = sbt("g0f32", [128, 512], F32, gub_off)
    qhi8 = sbt("qhi8", [128, NI], I8, a5.take(8192))
    stg5 = sbt("stg5", [128, 4 * 512], F32, a5.take(8192))

    psA = es.enter_context(nc.psum_tensor("psA", [128, 512], F32)).ap()
    psB = es.enter_context(nc.psum_tensor("psB", [128, 512], F32)).ap()
    psC = es.enter_context(nc.psum_tensor("psC", [128, 512], F32)).ap()
    psDp = es.enter_context(nc.psum_tensor("psD", [128, 512], F32)).ap()
    psE = es.enter_context(nc.psum_tensor("psE", [128, 512], F32)).ap()
    psF = es.enter_context(nc.psum_tensor("psF", [128, 512], F32)).ap()
    psT = [es.enter_context(nc.psum_tensor("psT1", [128, 1024], BF16)).ap(),
           es.enter_context(nc.psum_tensor("psT2", [128, 1024], BF16)).ap()]

    smQ = Sem(es, nc, "smQ")
    smG = Sem(es, nc, "smG")
    smPE = Sem(es, nc, "smPE")
    smDV = Sem(es, nc, "smDV")
    smSC = Sem(es, nc, "smSC")
    smOut = Sem(es, nc, "smOut")

    def wt(eng, sem, v):
        assert 0 <= v <= sem.n, f"wait {sem.name}>={v} but emitted {sem.n}"
        if v > 0:
            eng.wait_ge(sem.h, v)

    block = es.enter_context(nc.Block())

    @block.gpsimd
    def _(_gp):
        sy, sc, te, gp = nc.sync, nc.scalar, nc.tensor, nc.gpsimd
        _realve = nc.vector
        onescol = tritc[:, 127:128]

        _auto = set()
        _auto_refs = []
        _serial = {"on": False}

        class _VE:
            def __getattr__(self, name):
                fn = getattr(_realve, name)
                if name in ("wait_ge", "engine_nop"):
                    return fn

                def wrap(*a, **k):
                    ins = fn(*a, **k)
                    smDV.n += 1
                    ins.then_inc(smDV.h, 1)
                    _auto.add(id(ins))
                    _auto_refs.append(ins)
                    if _serial["on"]:
                        _realve.wait_ge(smDV.h, smDV.n)
                    return ins
                return wrap

        ve = _VE()

        class vserial:
            def __enter__(self):
                self.prev = _serial["on"]
                _serial["on"] = True

            def __exit__(self, *a):
                _serial["on"] = self.prev

        def vsw():
            _realve.wait_ge(smDV.h, smDV.n)
        _orig_inc = smDV.inc

        def _smdv_inc(ins, k=1):
            if id(ins) in _auto:
                return smDV.n
            return _orig_inc(ins, k)
        smDV.inc = _smdv_inc

        def vred(*a, **k):
            _realve.wait_ge(smDV.h, smDV.n)
            ins = ve.tensor_reduce(*a, **k)
            _realve.wait_ge(smDV.h, smDV.n)
            return ins

        def vrec(*a):
            _realve.wait_ge(smDV.h, smDV.n)
            ins = ve.reciprocal(*a)
            _realve.wait_ge(smDV.h, smDV.n)
            return ins

        def vrec2(dst, src, t1, t2):
            vrec(dst, src)
            ve.tensor_tensor(t1, src, dst, ALU.mult)
            vsw()
            ve.tensor_scalar(t2, t1, -1.0, 2.0, ALU.mult, ALU.add)
            vsw()
            ve.tensor_tensor(dst, dst, t2, ALU.mult)
            vsw()

        def qdma(out_, in_):
            return smQ.inc(sy.dma_start(out=out_, in_=in_), 16)

        def gdma(out_, in_):
            return smG.inc(gp.dma_start(out=out_, in_=in_), 16)

        def phase_barrier():
            for e in (gp, sy, ve, sc, te):
                wt(e, smPE, smPE.n)
                wt(e, smDV, smDV.n)
                wt(e, smSC, smSC.n)

        ic0 = gdma(identf, ident_in)
        wt(ve, smG, ic0)
        icv = smDV.inc(ve.tensor_copy(identb, identf))
        wt(gp, smDV, icv)
        gdma(diffc, diff_in)
        gdma(tritc, trit_in)
        gdma(colc, col_in)
        gdma(revc, rev_in)
        gdma(s_b, sb_in)
        gdma(qnw_b, qnw_in)
        gdma(nwcm, nwcm_in)
        gdma(pnwcm, pnwcm_in)
        c_init = smG.n
        ve.memset(S_all, 0.0)
        ve.memset(macc, 0.0)
        wt(ve, smG, c_init)
        _serial["on"] = True
        ve.memset(rt[28], EPS_LN)
        ve.memset(rt[29], EPS_E)
        ve.memset(rt[30], EPS_R)
        for h in range(H):
            v = ve.tensor_scalar(negs[h], s_b[:, h:h + 1], -1.0, None, ALU.mult)
        smDV.inc(v)
        _serial["on"] = False

        mviews = [wqkvT.rearrange("(a p) c -> a p c", p=128),
                  wprojT.rearrange("(a p) c -> a p c", p=128),
                  wgT.rearrange("(a p) c -> a p c", p=128),
                  wuT.rearrange("(a p) c -> a p c", p=128),
                  wdT.rearrange("(a p) c -> a p c", p=128)]
        mtiles = []
        for m, mv in enumerate(mviews):
            for a in range(mv.shape[0]):
                for j in range(mv.shape[2] // 2048):
                    mtiles.append((m, a, j))
        mst = {"dma_i": 0, "red_i": 0, "pend": [], "qcnt": [], "bcast": [0] * 5}

        def mean_step(k):
            for _ in range(k):
                if mst["dma_i"] < len(mtiles):
                    i = mst["dma_i"]
                    m, a, j = mtiles[i]
                    slot = i % 2
                    if i >= 2:
                        wt(sy, smDV, mst["pend"][i - 2])
                    c = qdma(mstage[:, slot * 2048:(slot + 1) * 2048],
                             mviews[m][a, :, j * 2048:(j + 1) * 2048])
                    mst["qcnt"].append(c)
                    mst["dma_i"] += 1
                if mst["red_i"] < mst["dma_i"]:
                    i = mst["red_i"]
                    slot = i % 2
                    wt(ve, smQ, mst["qcnt"][i])
                    vred(rt[20],
                                     mstage[:, slot * 2048:(slot + 1) * 2048],
                                     AX, ALU.add, apply_absolute_value=True)
                    m = mtiles[i][0]
                    v = ve.tensor_tensor(macc[:, m:m + 1], macc[:, m:m + 1],
                                         rt[20], ALU.add)
                    mst["pend"].append(smDV.inc(v))
                    mst["red_i"] += 1

        def mean_finalize(m):
            idx = max(i for i, t in enumerate(mtiles) if t[0] == m)
            assert mst["red_i"] > idx, f"matrix {m} mean not reduced yet"
            wt(te, smDV, smDV.n)
            mm = te.matmul(psF[0:1, 0:1], macc[:, m:m + 1], onescol, start=True, stop=True)
            pc = smPE.inc(mm)
            wt(ve, smPE, pc)
            _serial["on"] = True
            ve.tensor_scalar(rt[21][0:1, :], psF[0:1, 0:1], 1.0 / W_ELEMS[m], None, ALU.mult)
            ve.tensor_scalar(rt[22][0:1, :], rt[21][0:1, :], QEPS, None, ALU.max)
            vrec2(rt[23][0:1, :], rt[22][0:1, :], rt[17][0:1, :], rt[18][0:1, :])
            ve.tensor_copy(svec[0:1, 2 * m:2 * m + 1], rt[23][0:1, :])
            v = ve.tensor_copy(svec[0:1, 2 * m + 1:2 * m + 2], rt[22][0:1, :])
            _serial["on"] = False
            dc = smDV.inc(v)
            wt(te, smDV, dc)
            wt(gp, smDV, dc)
            c1 = gdma(bscr[0:1, 2 * m:2 * m + 2], svec[0:1, 2 * m:2 * m + 2])
            wt(gp, smG, c1)
            gdma(scb[2 * m], bscr[0:1, 2 * m:2 * m + 1].partition_broadcast(128))
            mst["bcast"][m] = gdma(scb[2 * m + 1],
                                   bscr[0:1, 2 * m + 1:2 * m + 2].partition_broadcast(128))

        def scale_ap(m):
            return scb[2 * m]

        trs = {"bank": 0, "evc": [0, 0]}

        def pe_transpose(src_bf16):
            b = trs["bank"]
            wt(te, smDV, trs["evc"][b])
            mm = te.transpose(psT[b][:, 0:128], src_bf16, identb)
            pc = smPE.inc(mm)
            trs["bank"] = 1 - b
            return psT[b][:, 0:128], pc, b

        def tr_evac(dst, ap, pc, b, scale=None):
            wt(ve, smPE, pc)
            v = ve.tensor_copy(dst, ap)
            if scale is not None:
                v = ve.tensor_scalar(dst, dst, scale, None, ALU.mult)
            trs["evc"][b] = smDV.inc(v)
            return trs["evc"][b]

        def ternarize(dst_bf, src_f32, m, i8buf, ncols):
            wt(ve, smG, mst["bcast"][m])
            step = i8buf.shape[1]
            last = 0
            for o in range(0, ncols, step):
                w = min(step, ncols - o)
                ve.tensor_scalar(src_f32[:, o:o + w], src_f32[:, o:o + w],
                                 scale_ap(m), 1.25, ALU.mult, ALU.min)
                vsw()
                ve.tensor_scalar(i8buf[:, 0:w], src_f32[:, o:o + w],
                                 -1.25, None, ALU.max)
                vsw()
                v = ve.tensor_copy(dst_bf[:, o:o + w], i8buf[:, 0:w])
                last = smDV.inc(v)
            return last

        ph1 = {"cons": {}, "hpe": 0}

        def emit_ln_chunk(tc_, prefix, dstH):
            slot = tc_ % 2
            xsrc = x_pre if prefix else x_own
            prev = ph1["cons"].get((prefix, tc_ - 2))
            if prev:
                wt(gp, smDV, prev[0])
                wt(gp, smSC, prev[1])
            xc = gdma(xst[:, slot * CD:(slot + 1) * CD],
                      xsrc[tc_ * 128:(tc_ + 1) * 128, :])
            xs = xst[:, slot * CD:(slot + 1) * CD]
            _serial["on"] = True
            wt(ve, smG, xc)
            vred(rt[0], xs, AX, ALU.add)
            wt(sc, smG, xc)
            wt(sc, smPE, ph1["hpe"])
            s1 = sc.activation(htmp, xs, AF.Square, accum_out=rt[1])
            scc = smSC.inc(s1)
            ve.tensor_scalar(rt[2], rt[0], 1.0 / CD, None, ALU.mult)
            ve.tensor_tensor(rt[3], rt[2], rt[2], ALU.mult)
            wt(ve, smSC, scc)
            ve.tensor_scalar(rt[4], rt[1], 1.0 / CD, None, ALU.mult)
            v = ve.tensor_tensor(rt[4], rt[4], rt[3], ALU.subtract)
            dv1 = smDV.inc(v)
            wt(sc, smDV, dv1)
            sc2 = smSC.inc(sc.activation(rt[5], rt[4], AF.Sqrt, bias=rt[28]))
            wt(ve, smSC, sc2)
            vrec(rt[6], rt[5])
            ve.tensor_scalar(xn, xs, rt[2], None, ALU.subtract)
            ve.tensor_scalar(xn, xn, rt[6], None, ALU.mult)
            ve.tensor_tensor(rt[7], rt[4], rt[6], ALU.mult)
            v = ve.tensor_tensor(rt[7], rt[7], rt[6], ALU.mult)
            dv2 = smDV.inc(v)
            wt(sc, smDV, dv2)
            sc3 = smSC.inc(sc.activation(rt[8], rt[7], AF.Sqrt, bias=rt[29]))
            wt(ve, smSC, sc3)
            vrec(rt[9], rt[8])
            v = ve.scalar_tensor_tensor(htmp, xn, rt[9], qnw_b, ALU.mult, ALU.mult)
            dvh = smDV.inc(v)
            wt(te, smDV, dvh)
            for fcc in range(FC):
                ap, pc, b = pe_transpose(htmp[:, fcc * 128:(fcc + 1) * 128])
                tr_evac(dstH[:, fcc * 2048 + tc_ * 128: fcc * 2048 + (tc_ + 1) * 128],
                        ap, pc, b)
            ph1["hpe"] = smPE.n
            if not prefix:
                vred(rt[10], xn, AX, ALU.max, apply_absolute_value=True)
                ve.tensor_scalar(rt[11], rt[10], QEPS, None, ALU.max)
                vrec2(rt[12], rt[11], rt[17], rt[18])
                ve.tensor_scalar(rt[13], rt[12], 127.0, None, ALU.mult)
                ve.tensor_scalar(inva[tc_], rt[11], 1.0 / 127.0, None, ALU.mult)
                ve.tensor_scalar(qxi8, xn, rt[13], None, ALU.mult)
                wt(ve, smPE, ph1["hpe"])
                dvq = smDV.inc(ve.tensor_copy(htmp, qxi8))
                wt(te, smDV, dvq)
                for fcc in range(FC):
                    ap, pc, b = pe_transpose(htmp[:, fcc * 128:(fcc + 1) * 128])
                    tr_evac(qxstg[:, fcc * 128:(fcc + 1) * 128], ap, pc, b)
                ph1["hpe"] = smPE.n
                wt(gp, smDV, smDV.n)
                gdma(qx_cache[tc_].rearrange("(c p) t -> p c t", p=128),
                     qxstg.rearrange("p (c t) -> p c t", c=FC))
            _serial["on"] = False
            ph1["cons"][(prefix, tc_)] = (smDV.n, smSC.n)

        at = {"wq": 0, "qkv_sc": 0, "qkv_dv": 0}

        def emit_attn_head(h, prefix):
            wst = wst2 if prefix else wst3
            wtern = wtern2 if prefix else wtern3
            wi8 = wi82 if prefix else wi83
            msk = mask2 if prefix else mask3
            ktp = ktmp2 if prefix else ktmp3
            hsrc = A if prefix else B
            ncolw = 256 if prefix else 384
            base = h * 384 + (128 if prefix else 0)
            wt(ve, smPE, smPE.n)
            wt(sc, smDV, smDV.n)
            nh = 1 if prefix else 2
            ck = FC // nh
            for hf in range(nh):
                wt(sy, smDV, at["wq"])
                wc = qdma(wst[:, 0:ck * ncolw].rearrange("p (c o) -> p c o", c=ck),
                          wqkvT[hf * ck * 128:(hf + 1) * ck * 128,
                                base:base + ncolw].rearrange("(c p) o -> p c o", p=128))
                wt(ve, smQ, wc)
                at["wq"] = ternarize(wtern[:, hf * ck * ncolw:(hf + 1) * ck * ncolw],
                                     wst[:, 0:ck * ncolw], W_QKV, wi8, ck * ncolw)
            nsh = negs[h]
            if not prefix:
                sc.activation(msk[:, 0:128], diffc, AF.Exp, scale=nsh)
                sc.activation(msk[:, 128:256], colc, AF.Exp, scale=nsh)
            sc.activation(msk[:, 256:384], revc, AF.Exp, scale=nsh)
            scm = smSC.inc(sc.activation(rt[16], s_b[:, h:h + 1], AF.Exp,
                                         scale=-128.0))
            wt(ve, smSC, scm)
            if not prefix:
                smDV.inc(ve.tensor_tensor(msk[:, 0:128], msk[:, 0:128], tritc, ALU.mult))
                vsw()
            js = (1, 2) if prefix else (0, 1, 2)
            dstb = kvfm if prefix else qkvb
            pss = [psA, psB, psC]
            wt(te, smDV, at["wq"])
            wt(te, smSC, scm)
            for tb in range(4):
                for ji, j in enumerate(js):
                    off = j - (1 if prefix else 0)
                    for kc in range(FC):
                        mm = te.matmul(
                            pss[ji][:, 0:512],
                            wtern[:, kc * ncolw + off * 128: kc * ncolw + off * 128 + 128],
                            hsrc[:, kc * 2048 + tb * 512: kc * 2048 + tb * 512 + 512],
                            start=(kc == 0), stop=(kc == FC - 1))
                    pc = smPE.inc(mm)
                    dst = dstb[:, off * 2048 + tb * 512: off * 2048 + tb * 512 + 512]
                    if ji % 2 == 0:
                        wt(sc, smPE, pc)
                        wt(sc, smG, mst["bcast"][W_QKV])
                        at["qkv_sc"] = smSC.inc(sc.activation(dst, pss[ji][:, 0:512], AF.Copy,
                                                              scale=scb[2 * W_QKV + 1]))
                    else:
                        wt(ve, smPE, pc)
                        at["qkv_dv"] = smDV.inc(ve.tensor_scalar(dst, pss[ji][:, 0:512],
                                                                 scb[2 * W_QKV + 1], None, ALU.mult))
                wt(te, smSC, at["qkv_sc"])
                wt(te, smDV, at["qkv_dv"])
            koff = 0 if prefix else 2048
            voff = 2048 if prefix else 4096
            qt_sl = ktp[:, 4 * 128:5 * 128]
            sb_sl = ktp[:, 5 * 128:6 * 128]
            for c in range(TC):
                cb = c * 128
                ap, pc, b = pe_transpose(dstb[:, voff + cb: voff + cb + 128])
                tr_evac(ktp[:, 0:128], ap, pc, b)
                v = ve.tensor_tensor(ktp[:, 128:256],
                                     dstb[:, koff + cb: koff + cb + 128],
                                     msk[:, 256:384], ALU.mult)
                kdv = smDV.inc(v)
                wt(te, smDV, kdv)
                ap2, pc2, b2 = pe_transpose(ktp[:, 128:256])
                kevc = tr_evac(ktp[:, 256:384], ap2, pc2, b2)
                if not prefix:
                    mm = te.matmul(psDp[:, 0:128], dstb[:, 2048 + cb: 2048 + cb + 128],
                                   dstb[:, cb: cb + 128], start=True, stop=True)
                    pcs = smPE.inc(mm)
                    wt(ve, smPE, pcs)
                    ve.tensor_tensor(ktp[:, 384:512], psDp[:, 0:128],
                                     msk[:, 0:128], ALU.mult)
                    ve.tensor_tensor(qt_sl, dstb[:, cb: cb + 128],
                                     msk[:, 128:256], ALU.mult)
                    dsb = smDV.inc(ve.tensor_copy(sb_sl, S_all[:, h * 128:(h + 1) * 128]))
                    wt(te, smDV, dsb)
                    te.matmul(psE[:, 0:128], ktp[:, 384:512], ktp[:, 0:128],
                              start=True, stop=False)
                    mm = te.matmul(psE[:, 0:128], qt_sl, sb_sl, start=False, stop=True)
                    pco = smPE.inc(mm)
                    wt(sc, smPE, pco)
                    smSC.inc(sc.activation(A[:, c * 2048 + h * 128: c * 2048 + (h + 1) * 128],
                                           psE[:, 0:128], AF.Copy, scale=ATT_SCALE))
                wt(te, smDV, kevc)
                mm = te.matmul(psF[:, 0:128], ktp[:, 256:384], ktp[:, 0:128],
                               start=True, stop=True)
                pkv = smPE.inc(mm)
                wt(ve, smPE, pkv)
                v = ve.scalar_tensor_tensor(S_all[:, h * 128:(h + 1) * 128],
                                            S_all[:, h * 128:(h + 1) * 128],
                                            rt[16], psF[:, 0:128],
                                            ALU.mult, ALU.add)
                dkv = smDV.inc(v)
                vsw()
                wt(te, smDV, dkv)
                wt(te, smSC, smSC.n)

        for tci in range(TC):
            emit_ln_chunk(tci, True, A)
            mean_step(3)
        mean_finalize(W_QKV)
        phase_barrier()

        for h in range(H):
            emit_attn_head(h, True)
            mean_step(6)
        mean_finalize(W_PROJ)
        phase_barrier()

        gdma(qnw_b, qnw_in)
        for tci in range(TC):
            emit_ln_chunk(tci, False, B)
            mean_step(5)
        mean_finalize(W_G)
        qx_done = smG.n
        phase_barrier()

        if upto >= 2:
            for h in range(H):
                emit_attn_head(h, False)
                mean_step(9)
            mean_step(len(mtiles))
            mean_finalize(W_U)
            mean_finalize(W_D)
            phase_barrier()

        scratch_done = 0
        if upto >= 3:
            wt(gp, smG, mst["bcast"][W_PROJ])
            xc = gdma(sq4, nw_in)
            wt(sc, smG, xc)
            scn = smSC.inc(sc.activation(nw2b, sq4, AF.Square))
            _serial["on"] = True
            for fcc in range(FC):
                v = ve.tensor_tensor(cvc[fcc], nwcm[:, fcc:fcc + 1],
                                     pnwcm[:, fcc:fcc + 1], ALU.mult)
            smDV.inc(v)
            _serial["on"] = False
            ph4sc = {"sq": 0}
            for tc_ in range(TC):
                asl = A[:, tc_ * 2048:(tc_ + 1) * 2048]
                wt(sc, smDV, ph4sc["sq"])
                if tc_ == 0:
                    wt(sc, smSC, scn)
                s1 = sc.activation(sq4, asl, AF.Square, accum_out=rt[1])
                scc = smSC.inc(s1)
                wt(ve, smSC, max(scc, scn))
                _serial["on"] = True
                ve.tensor_tensor(t24, sq4, nw2b, ALU.mult)
                vred(rt[2], t24, AX, ALU.add)
                ve.tensor_scalar(rt[3], rt[1], 1.0 / CD, None, ALU.mult)
                v = ve.tensor_scalar(rt[4], rt[2], 1.0 / CD, None, ALU.mult)
                dv1 = smDV.inc(v)
                ph4sc["sq"] = dv1
                wt(sc, smDV, dv1)
                sc2 = smSC.inc(sc.activation(rt[5], rt[3], AF.Sqrt, bias=rt[30]))
                wt(ve, smSC, sc2)
                vrec(rt[6], rt[5])
                ve.tensor_tensor(rt[7], rt[6], rt[6], ALU.mult)
                v = ve.tensor_tensor(rt[8], rt[4], rt[7], ALU.mult)
                dv2 = smDV.inc(v)
                wt(sc, smDV, dv2)
                sc3 = smSC.inc(sc.activation(rt[9], rt[8], AF.Sqrt, bias=rt[29]))
                wt(ve, smSC, sc3)
                vrec(rt[10], rt[9])
                v = ve.tensor_tensor(rt[11], rt[6], rt[10], ALU.mult)
                dv3 = smDV.inc(v)
                _serial["on"] = False
                wt(sc, smDV, dv3)
                s4 = sc.activation(sct4, asl, AF.Copy, scale=rt[11])
                sc4 = smSC.inc(s4)
                wt(te, smSC, sc4)
                for fcc in range(FC):
                    ap, pc, b = pe_transpose(sct4[:, fcc * 128:(fcc + 1) * 128])
                    tr_evac(B[:, fcc * 2048 + tc_ * 128: fcc * 2048 + (tc_ + 1) * 128],
                            ap, pc, b, scale=cvc[fcc])
            ph4w = {"tern": 0}
            slotdma = {0: 0, 1: 0, 2: 0, 3: 0}
            for ob in range(4):
                wt(ve, smPE, smPE.n)
                for kc in range(FC):
                    slot = kc % 2
                    wt(sy, smDV, ph4w["tern"])
                    wcc = qdma(wst4[:, slot * 512:(slot + 1) * 512],
                               wprojT[kc * 128:(kc + 1) * 128, ob * 512:(ob + 1) * 512])
                    wt(ve, smQ, wcc)
                    wt(ve, smG, mst["bcast"][W_PROJ])
                    ve.tensor_scalar(wst4[:, slot * 512:(slot + 1) * 512],
                                     wst4[:, slot * 512:(slot + 1) * 512],
                                     scale_ap(W_PROJ), 1.25, ALU.mult, ALU.min)
                    vsw()
                    ve.tensor_scalar(wi84, wst4[:, slot * 512:(slot + 1) * 512],
                                     -1.25, None, ALU.max)
                    vsw()
                    v = ve.tensor_copy(ternP[:, kc * 512:(kc + 1) * 512], wi84)
                    ph4w["tern"] = smDV.inc(v)
                wt(te, smDV, ph4w["tern"])
                evp = {"sc": 0, "dv": 0}
                for tc_ in range(TC):
                    pst = psA if tc_ % 2 == 0 else psB
                    for kc in range(FC):
                        mm = te.matmul(pst[:, 0:512],
                                       B[:, kc * 2048 + tc_ * 128: kc * 2048 + (tc_ + 1) * 128],
                                       ternP[:, kc * 512:(kc + 1) * 512],
                                       start=(kc == 0), stop=(kc == FC - 1))
                    pc = smPE.inc(mm)
                    xslot = tc_ % 2
                    xc = gdma(stg4[:, xslot * 512:(xslot + 1) * 512],
                              x_own[tc_ * 128:(tc_ + 1) * 128, ob * 512:(ob + 1) * 512])
                    tslot = tc_ % 4
                    wt(ve, smG, xc)
                    wt(ve, smG, slotdma[tslot])
                    wt(ve, smPE, pc)
                    v = ve.scalar_tensor_tensor(t24[:, tslot * 512:(tslot + 1) * 512],
                                                pst[:, 0:512], scb[2 * W_PROJ + 1],
                                                stg4[:, xslot * 512:(xslot + 1) * 512],
                                                ALU.mult, ALU.add)
                    dvv = smDV.inc(v)
                    wt(te, smDV, dvv)
                    wt(gp, smDV, dvv)
                    slotdma[tslot] = gdma(
                        scratch[tc_ * 128:(tc_ + 1) * 128, ob * 512:(ob + 1) * 512],
                        t24[:, tslot * 512:(tslot + 1) * 512])
            scratch_done = smG.n
            phase_barrier()

        if upto >= 4:
            wt(gp, smG, mst["bcast"][W_G])
            wt(gp, smG, mst["bcast"][W_U])
            wt(gp, smG, mst["bcast"][W_D])
            wt(gp, smG, qx_done)
            wt(gp, smG, scratch_done)
            lc = gdma(lnwbf, lnw_in)
            wt(ve, smG, lc)
            _serial["on"] = True
            for i in range(TC):
                ve.tensor_scalar(invg[i], inva[i], scb[2 * W_G + 1], None, ALU.mult)
                v = ve.tensor_scalar(invu[i], inva[i], scb[2 * W_U + 1], None, ALU.mult)
            smDV.inc(v)
            _serial["on"] = False
            cache_ready = {W_G: 0, W_U: 0, W_D: 0}
            slotst = {"pe": [0, 0], "i8dv": 0}
            scrfree = {0: 0, 1: 0}

            def load_rhs_slot(m, slot, kc0, nkc, ob, group):
                dst = ternW[:, slot * 2048: slot * 2048 + nkc * 512]
                wt(gp, smPE, slotst["pe"][slot])
                wt(sy, smPE, slotst["pe"][slot])
                src = wsrc[m]
                cache = ternCache[m]
                if group == 0:
                    f32b = g0f32 if m == W_D else stg5
                    nslots = 1 if m == W_D else 2
                    if m == W_D:
                        wt(sy, smDV, smDV.n)
                    last = slotst.get("g0dv", 0)
                    for i in range(nkc):
                        kc = kc0 + i
                        s2 = i % nslots
                        wt(sy, smDV, last)
                        wcc = qdma(f32b[:, s2 * 512:(s2 + 1) * 512],
                                   src[kc * 128:(kc + 1) * 128, ob * 512:(ob + 1) * 512])
                        wt(ve, smQ, wcc)
                        wt(ve, smG, mst["bcast"][m])
                        wt(ve, smG, slotst["i8dv"])
                        ve.tensor_scalar(f32b[:, s2 * 512:(s2 + 1) * 512],
                                         f32b[:, s2 * 512:(s2 + 1) * 512],
                                         scale_ap(m), 1.25, ALU.mult, ALU.min)
                        vsw()
                        ve.tensor_scalar(qhi8[:, i * 512:(i + 1) * 512],
                                         f32b[:, s2 * 512:(s2 + 1) * 512],
                                         -1.25, None, ALU.max)
                        vsw()
                        v = ve.tensor_copy(dst[:, i * 512:(i + 1) * 512],
                                           qhi8[:, i * 512:(i + 1) * 512])
                        last = smDV.inc(v)
                        wt(gp, smDV, last)
                        cache_ready[m] = gdma(
                            cache[kc * 128:(kc + 1) * 128, ob * 512:(ob + 1) * 512],
                            qhi8[:, i * 512:(i + 1) * 512])
                    slotst["i8dv"] = cache_ready[m]
                    slotst["g0dv"] = last
                    return ("dv", last)
                else:
                    wt(gp, smG, cache_ready[m])
                    c = gdma(dst.rearrange("p (c o) -> p c o", c=nkc),
                             cache[kc0 * 128:(kc0 + nkc) * 128,
                                   ob * 512:(ob + 1) * 512].rearrange(
                                       "(c p) o -> p c o", p=128))
                    return ("g", c)

            for g in range(NG):
                wt(gp, smPE, smPE.n)
                for t_ in range(GT):
                    tc_ = g * GT + t_
                    gdma(qxg[:, t_ * 2048:(t_ + 1) * 2048].rearrange(
                        "p (c t) -> p c t", c=FC),
                        qx_cache[tc_].rearrange("(c p) t -> p c t", p=128))
                qxc = smG.n
                psGU = [psA, psB, psC, psDp]
                for ob in range(FC):
                    hoff = ob * 512
                    for m in (W_G, W_U):
                        wt(te, smG, qxc)
                        for si in range(4):
                            slot = si % 2
                            kind, val = load_rhs_slot(m, slot, si * 4, 4, ob, g)
                            wt(te, smG if kind == "g" else smDV, val)
                            for t_ in range(GT):
                                for i in range(4):
                                    kc = si * 4 + i
                                    mm = te.matmul(
                                        psGU[t_][:, 0:512],
                                        qxg[:, t_ * 2048 + kc * 128: t_ * 2048 + (kc + 1) * 128],
                                        ternW[:, slot * 2048 + i * 512: slot * 2048 + (i + 1) * 512],
                                        start=(kc == 0), stop=(kc == FC - 1))
                            pc = smPE.inc(mm)
                            slotst["pe"][slot] = pc
                        pc = smPE.n
                        for t_ in range(GT):
                            if m == W_G:
                                wt(sc, smPE, pc)
                                wt(sc, smDV, smDV.n)
                                s1 = sc.activation(gub[:, t_ * 512:(t_ + 1) * 512],
                                                   psGU[t_][:, 0:512], AF.Silu,
                                                   scale=invg[g * GT + t_])
                                scc = smSC.inc(s1)
                                wt(te, smSC, scc)
                            else:
                                wt(ve, smPE, pc)
                                wt(ve, smSC, smSC.n)
                                v = ve.scalar_tensor_tensor(
                                    A[:, t_ * NI + hoff: t_ * NI + hoff + 512],
                                    psGU[t_][:, 0:512],
                                    invu[g * GT + t_],
                                    gub[:, t_ * 512:(t_ + 1) * 512],
                                    ALU.mult, ALU.mult)
                                dvv = smDV.inc(v)
                                wt(te, smDV, dvv)
                for t_ in range(GT):
                    asl = A[:, t_ * NI:(t_ + 1) * NI]
                    bsl = B[:, t_ * NI:(t_ + 1) * NI]
                    wt(sc, smDV, smDV.n)
                    s1 = sc.activation(bsl, asl, AF.Square, accum_out=rt[1])
                    scc = smSC.inc(s1)
                    wt(ve, smSC, scc)
                    _serial["on"] = True
                    v = ve.tensor_scalar(rt[2], rt[1], 1.0 / NI, None, ALU.mult)
                    dv1 = smDV.inc(v)
                    wt(sc, smDV, dv1)
                    sc2 = smSC.inc(sc.activation(rt[3], rt[2], AF.Sqrt,
                                                 bias=rt[30]))
                    wt(ve, smSC, sc2)
                    vrec(rt[4], rt[3])
                    ve.scalar_tensor_tensor(asl, asl, rt[4], lnwbf,
                                            ALU.mult, ALU.mult)
                    vred(rt[5], asl, AX, ALU.max,
                                     apply_absolute_value=True)
                    ve.tensor_scalar(rt[6], rt[5], QEPS, None, ALU.max)
                    vrec2(rt[7], rt[6], rt[17], rt[18])
                    ve.tensor_scalar(rt[8], rt[7], 127.0, None, ALU.mult)
                    ve.tensor_scalar(rt[9], rt[6], 1.0 / 127.0, None, ALU.mult)
                    ve.tensor_scalar(invd[g * GT + t_], rt[9],
                                     scb[2 * W_D + 1], None, ALU.mult)
                    wt(ve, smG, slotst["i8dv"])
                    ve.tensor_scalar(qhi8, asl, rt[8], None, ALU.mult)
                    v = ve.tensor_copy(asl, qhi8)
                    _serial["on"] = False
                    dvq = smDV.inc(v)
                    wt(te, smDV, dvq)
                    for ic in range(64):
                        ap, pc, b = pe_transpose(asl[:, ic * 128:(ic + 1) * 128])
                        tr_evac(bsl[:, ic * 128:(ic + 1) * 128], ap, pc, b)
                qh_dv = smDV.n
                psDn = [psA, psB, psC, psDp]
                for ob in range(4):
                    for si in range(16):
                        slot = si % 2
                        kind, val = load_rhs_slot(W_D, slot, si * 4, 4, ob, g)
                        wt(te, smG if kind == "g" else smDV, val)
                        if si == 0:
                            wt(te, smDV, qh_dv)
                        for t_ in range(GT):
                            for i in range(4):
                                ic = si * 4 + i
                                mm = te.matmul(
                                    psDn[t_][:, 0:512],
                                    B[:, t_ * NI + ic * 128: t_ * NI + (ic + 1) * 128],
                                    ternW[:, slot * 2048 + i * 512: slot * 2048 + (i + 1) * 512],
                                    start=(ic == 0), stop=(ic == 63))
                        pc = smPE.inc(mm)
                        slotst["pe"][slot] = pc
                    wt(ve, smPE, smPE.n)
                    for t_ in range(GT):
                        tc_ = g * GT + t_
                        ssl = t_ % 2
                        wt(gp, smDV, scrfree[ssl])
                        xc = gdma(stg5[:, ssl * 512:(ssl + 1) * 512],
                                  scratch[tc_ * 128:(tc_ + 1) * 128,
                                          ob * 512:(ob + 1) * 512])
                        wt(ve, smG, xc)
                        wt(ve, smOut, smOut.n)
                        v = ve.scalar_tensor_tensor(
                            stg5[:, (2 + ssl) * 512:(3 + ssl) * 512],
                            psDn[t_][:, 0:512],
                            invd[tc_],
                            stg5[:, ssl * 512:(ssl + 1) * 512],
                            ALU.mult, ALU.add)
                        dvv = smDV.inc(v)
                        scrfree[ssl] = dvv
                        wt(te, smDV, dvv)
                        wt(gp, smDV, dvv)
                        smOut.inc(gp.dma_start(
                            out=out[tc_ * 128:(tc_ + 1) * 128, ob * 512:(ob + 1) * 512],
                            in_=stg5[:, (2 + ssl) * 512:(3 + ssl) * 512]), 16)

        if debug:
            wt(gp, smDV, smDV.n)
            wt(gp, smSC, smSC.n)
            wt(gp, smPE, smPE.n)
            if upto < 2:
                gdma(dbg["hpre"], A)
            gdma(dbg["hattn"], B)
            gdma(dbg["qx"], qx_cache)
            _serial["on"] = True
            for i in range(TC):
                v = ve.tensor_copy(dbg_invs[:, i:i + 1], inva[i])
            dvv = smDV.inc(v)
            _serial["on"] = False
            wt(gp, smDV, dvv)
            gdma(dbg["invs"], dbg_invs)
            gdma(dbg["S"], S_all)
            gdma(dbg["scal"], svec)
            gdma(dbg["macc"], macc)
            if 2 <= upto <= 3:
                gdma(dbg["aout"], A)
            if upto == 3:
                gdma(dbg["hattn"], B)
                gdma(dbg["ternp"], ternP)
            if upto >= 3:
                gdma(dbg["scr"], scratch)
            wt(gp, smG, smG.n)
        if upto >= 4:
            wt(gp, smOut, smOut.n)
        else:
            ve.memset(xn, 0.0)
            dz = smDV.inc(ve.tensor_scalar(xn, xn, 1.0, None, ALU.mult))
            wt(gp, smDV, dz)
            for tci in range(TC):
                smOut.inc(gp.dma_start(out=out[tci * 128:(tci + 1) * 128, :], in_=xn), 16)
            wt(gp, smOut, smOut.n)

    return nc, es


def host_layout(inputs):
    x = np.asarray(inputs["L0_attn_x"], dtype=np.float32).reshape(16384, CD)
    qkv_w = np.asarray(inputs["L0_attn_qkv_w"], dtype=np.float32)
    proj_w = np.asarray(inputs["L0_attn_proj_w"], dtype=np.float32)
    w_gate = np.asarray(inputs["L0_ffn_w_gate"], dtype=np.float32)
    w_up = np.asarray(inputs["L0_ffn_w_up"], dtype=np.float32)
    w_down = np.asarray(inputs["L0_ffn_w_down"], dtype=np.float32)
    qnw = np.asarray(inputs["L0_attn_qkv_norm_w"], dtype=np.float32)
    pnw = np.asarray(inputs["L0_attn_proj_norm_w"], dtype=np.float32)
    nw = np.asarray(inputs["L0_attn_norm_w"], dtype=np.float32)
    s = np.asarray(inputs["L0_attn_s"], dtype=np.float32)
    lnw = np.asarray(inputs["L0_ffn_ln_weight"], dtype=np.float32)

    perm = np.empty(3 * CD, dtype=np.int64)
    i = 0
    for h in range(H):
        for j in range(3):
            perm[i:i + 128] = j * CD + h * 128 + np.arange(128)
            i += 128
    wqkvT_hm = np.ascontiguousarray(qkv_w[perm].T)

    shared = {
        "wqkvT": wqkvT_hm,
        "wprojT": np.ascontiguousarray(proj_w.T),
        "wgT": np.ascontiguousarray(w_gate.T),
        "wuT": np.ascontiguousarray(w_up.T),
        "wdT": np.ascontiguousarray(w_down.T),
        "qnw_b": np.ascontiguousarray(np.broadcast_to(qnw, (128, CD))),
        "lnw_b": np.ascontiguousarray(np.broadcast_to(lnw, (128, NI))),
        "nw_b": np.ascontiguousarray(np.broadcast_to(nw, (128, CD))),
        "nw_cm": np.ascontiguousarray(nw.reshape(FC, 128).T),
        "pnw_cm": np.ascontiguousarray(pnw.reshape(FC, 128).T),
        "s_b": np.ascontiguousarray(np.broadcast_to(s, (128, H))),
        "identf": np.eye(128, dtype=np.float32),
        "diffc": np.maximum(np.arange(128)[None, :] - np.arange(128)[:, None], 0).astype(np.float32),
        "tritc": (np.arange(128)[None, :] >= np.arange(128)[:, None]).astype(np.float32),
        "colc": np.ascontiguousarray(np.broadcast_to(np.arange(128, dtype=np.float32)[None, :], (128, 128))),
        "revc": np.ascontiguousarray(np.broadcast_to(128.0 - np.arange(128, dtype=np.float32)[None, :], (128, 128))),
    }
    zeros = np.zeros((TOK, CD), dtype=np.float32)
    in_maps = []
    for c in range(8):
        m = dict(shared)
        m["x_own"] = np.ascontiguousarray(x[c * TOK:(c + 1) * TOK])
        m["x_pre"] = (np.ascontiguousarray(x[(c - 1) * TOK:c * TOK])
                      if c % 2 == 1 else zeros)
        in_maps.append(m)
    return in_maps


def kernel(**inputs):
    in_maps = host_layout(inputs)
    nc, es = build_nc(upto=5, debug=False)
    try:
        res = run_bass_kernel_spmd(nc, in_maps, core_ids=list(range(8)))
    finally:
        es.close()
    full = np.concatenate([res.results[c]["out"] for c in range(8)], axis=0)
    return full.reshape(4, 4096, CD)
